# version 02 (final)
# baseline (speedup 1.0000x reference)
import contextlib
import os
import numpy as np
import ml_dtypes
import concourse.bass as bass
import concourse.mybir as mybir
from concourse.bass_utils import run_bass_kernel_spmd

F32 = mybir.dt.float32
BF16 = mybir.dt.bfloat16
I32 = mybir.dt.int32
AF = mybir.ActivationFunctionType
ALU = mybir.AluOpType
AX = mybir.AxisListType

B, S, E, H, D = 2, 8192, 2048, 16, 128
R = B * S
NCORE = 8
RPC = R // NCORE
INNER = 8192
NRT = RPC // 128
QB = 512
NJ = S // QB
EPS = float(np.float32(1e-5))
INV2048 = float(np.float32(1.0 / 2048.0))
INV8192 = float(np.float32(1.0 / 8192.0))
INVWN = float(np.float32(1.0 / (8192.0 * 2048.0)))
INV127 = float(np.float32(1.0 / 127.0))
INVSQD = float(np.float32(1.0) / np.sqrt(np.float32(128.0)))
INVSQ2 = float(np.float32(1.0 / np.sqrt(2.0)))

ENGINES = ("sync", "tensor", "vector", "scalar", "gpsimd")


NLANES = 4


class Prog:

    def __init__(self, nc, stack):
        self.nc = nc
        self.stack = stack
        self.recs = {e: [] for e in ENGINES}
        self.counts = {e: 0 for e in ENGINES}
        self.phase = 0
        self.base = {e: 0 for e in ENGINES}
        self.lanes = {e: [0] * (NLANES + 1) for e in ENGINES}
        self.ndma = {e: 0 for e in ENGINES}
        self.snap = {}
        self.final = {}
        self.prev_unit = {}
        self._sems = {}

    def sem(self, phase, eng, lane):
        key = (phase, eng, lane)
        if key not in self._sems:
            self._sems[key] = self.stack.enter_context(
                self.nc.semaphore(f"s{phase}_{eng}_{lane}"))
        return self._sems[key]

    def _record(self, eng, fn, waits, inc, lane):
        w = [(e, c) for (e, c) in waits if c > 0]
        if eng in ("vector", "scalar") and inc == 1:
            prev = self.prev_unit.get(eng, 0)
            if prev > self.base[eng] and all(
                    not (e == eng and c >= prev) for (e, c) in w):
                w.append((eng, prev))
        if inc == 1:
            self.prev_unit[eng] = self.counts[eng] + inc
        self.recs[eng].append((w, fn, inc, lane))
        self.counts[eng] += inc
        self.lanes[eng][lane] += inc
        self.snap[(eng, self.counts[eng])] = (self.phase, tuple(self.lanes[eng]))
        return self.counts[eng]

    def emit(self, eng, fn, waits=(), inc=1):
        return self._record(eng, fn, waits, inc, NLANES)

    def dma(self, eng, out, in_, waits=(), transpose=False):
        def fn(E, out=out, in_=in_, transpose=transpose):
            return E.dma_start(out=out, in_=in_, transpose=transpose)
        lane = self.ndma[eng] % NLANES
        self.ndma[eng] += 1
        return self._record(eng, fn, waits, inc=16, lane=lane)

    def barrier_waits(self):
        for e in ("sync", "scalar", "gpsimd"):
            c = self.counts[e]
            if c > self.base[e]:
                self._record(e, lambda E: None, [(e, c)], 0, NLANES)

    def _expand(self, e2, c):
        ph, lv = self.snap[(e2, c)]
        out = []
        if ph < self.phase:
            for p2 in range(ph, self.phase):
                fv = self.final.get((p2, e2))
                if fv is None:
                    continue
                for k, v in enumerate(fv):
                    if v > 0:
                        out.append(((p2, e2, k), v))
            return out
        for k, v in enumerate(lv):
            if v > 0:
                out.append(((self.phase, e2, k), v))
        return out

    def flush(self, nc, sems=None):
        needed = {}
        for eng in ENGINES:
            for w, fn, inc, lane in self.recs[eng]:
                for (e2, c) in w:
                    for key, v in self._expand(e2, c):
                        if key[2] < NLANES:
                            needed.setdefault(key, set()).add(v)
        with nc.Block() as block:
            for eng in ENGINES:
                recs = self.recs[eng]
                if not recs:
                    continue

                def body(Eng, recs=recs, eng=eng):
                    cur = {}
                    lc = [0] * (NLANES + 1)
                    for w, fn, inc, lane in recs:
                        seen = set()
                        for (e2, c) in w:
                            for key, v in self._expand(e2, c):
                                if (key, v) not in seen:
                                    Eng.wait_ge(self.sem(*key), v)
                                    seen.add((key, v))
                                    cur[key] = max(cur.get(key, -1), v)
                        r = fn(Eng)
                        if inc:
                            r.then_inc(self.sem(self.phase, eng, lane), inc)
                            lc[lane] += inc
                            key = (self.phase, eng, lane)
                            if lane < NLANES and lc[lane] in needed.get(key, ()):
                                Eng.wait_ge(self.sem(*key), lc[lane])
                                cur[key] = max(cur.get(key, -1), lc[lane])
                getattr(block, eng)(body)
        for e in ENGINES:
            self.final[(self.phase, e)] = tuple(self.lanes[e])
            self.lanes[e] = [0] * (NLANES + 1)
            self.base[e] = self.counts[e]
        self.recs = {e: [] for e in ENGINES}
        self.phase += 1


def build_program():
    nc = bass.Bass("TRN2", target_bir_lowering=False)

    x_in = nc.dram_tensor("x_rows", [RPC, E], F32, kind="ExternalInput").ap()
    wq_in = nc.dram_tensor("wq_s", [E, 512], F32, kind="ExternalInput").ap()
    wk_in = nc.dram_tensor("wk_s", [E, 512], F32, kind="ExternalInput").ap()
    wv_in = nc.dram_tensor("wv_s", [E, 512], F32, kind="ExternalInput").ap()
    wo_in = nc.dram_tensor("wo_s", [512, E], F32, kind="ExternalInput").ap()
    bo_in = nc.dram_tensor("bo", [1, E], F32, kind="ExternalInput").ap()
    w1_in = nc.dram_tensor("w1_s", [1024, E], F32, kind="ExternalInput").ap()
    w2_in = nc.dram_tensor("w2_s", [256, INNER], F32, kind="ExternalInput").ap()
    b1_in = nc.dram_tensor("b1", [1, INNER], F32, kind="ExternalInput").ap()
    g_in = nc.dram_tensor("ln_g", [1, INNER], F32, kind="ExternalInput").ap()
    bln_in = nc.dram_tensor("ln_b", [1, INNER], F32, kind="ExternalInput").ap()
    b2_in = nc.dram_tensor("b2", [1, E], F32, kind="ExternalInput").ap()
    btab_in = nc.dram_tensor("bias_tab", [128, 256], F32, kind="ExternalInput").ap()
    mask_in = nc.dram_tensor("masks", [128, 4 * QB], BF16, kind="ExternalInput").ap()
    onc_in = nc.dram_tensor("ones_col", [128, 1], F32, kind="ExternalInput").ap()
    onr_in = nc.dram_tensor("ones_row", [1, 128], F32, kind="ExternalInput").ap()
    y_out = nc.dram_tensor("y", [RPC, E], F32, kind="ExternalOutput").ap()
    dbg_out = nc.dram_tensor("dbg", [2048, 2048], BF16, kind="ExternalOutput").ap()

    cc_s_in = nc.dram_tensor("cc_s_in", [1, 2], F32)
    cc_s_out = nc.dram_tensor("cc_s_out", [8, 2], F32)
    cc_xn_in = nc.dram_tensor("cc_xn_in", [RPC, E], BF16)
    cc_xn_out = nc.dram_tensor("cc_xn_out", [S, E], BF16)
    nc.dram_tensor("guard1", [64 * 1024 * 1024], BF16)
    cc_w1_in = nc.dram_tensor("cc_w1_in", [1024, 2048], BF16)
    cc_w2_in = nc.dram_tensor("cc_w2_in", [256, 8192], BF16)
    cc_w1_out = nc.dram_tensor("cc_w1_out", [8 * 1024, 2048], BF16)
    cc_w2_out = nc.dram_tensor("cc_w2_out", [8 * 256, 8192], BF16)
    nc.dram_tensor("guard2", [64 * 1024 * 1024], BF16)
    ctx_scr = nc.dram_tensor("ctx_scr", [512, S], BF16)
    cc_rs_in = nc.dram_tensor("cc_rs_in", [S, E], F32)
    cc_rs_out = nc.dram_tensor("cc_rs_out", [RPC, E], F32)
    xq_scr = nc.dram_tensor("xq_scr", [RPC, E], BF16)
    a_spill = nc.dram_tensor("a_spill", [RPC, INNER], BF16)
    hq_spill = nc.dram_tensor("hq_spill", [RPC, INNER], BF16)
    w2t_scr = nc.dram_tensor("w2t_scr", [INNER, E], BF16)
    ffn_spill = nc.dram_tensor("ffn_spill", [RPC, E], F32)

    def w1q_view(rank):
        return cc_w1_out.ap()[rank * 1024:(rank + 1) * 1024, :]

    def w2q_view(rank):
        return cc_w2_out.ap()[rank * 256:(rank + 1) * 256, :]

    cw_in_w1 = cc_w1_in.ap()
    cw_in_w2 = cc_w2_in.ap()

    top = contextlib.ExitStack()
    with top:
        p = Prog(nc, top)
        sems = None

        stats = top.enter_context(nc.sbuf_tensor("stats", [128, 16 * 8], F32))
        SUMS, SSQ, MEAN, RSTD, EV1, RSQ1, MU2, RSTD2 = range(8)

        def stc(k, rt):
            return stats[:, k * 16 + rt:k * 16 + rt + 1]

        def sts(k):
            return stats[:, k * 16:(k + 1) * 16]

        tiny = top.enter_context(nc.sbuf_tensor("tiny", [128, 32], F32))
        onesc = top.enter_context(nc.sbuf_tensor("onesc", [128, 1], F32))
        onesr = top.enter_context(nc.sbuf_tensor("onesr", [1, 128], F32))
        ev2 = top.enter_context(nc.sbuf_tensor("ev2", [128, 16], F32))
        rsq2 = top.enter_context(nc.sbuf_tensor("rsq2", [128, 16], F32))
        fstats = top.enter_context(nc.sbuf_tensor("fstats", [128, 512], F32))

        stA = contextlib.ExitStack()
        with stA:
            wbuf = [stA.enter_context(nc.sbuf_tensor(f"wbuf{i}", [128, INNER], F32)) for i in range(2)]
            acc2 = stA.enter_context(nc.sbuf_tensor("acc2", [128, 2], F32))
            prow = stA.enter_context(nc.sbuf_tensor("prow", [1, 2], F32))
            psum8 = stA.enter_context(nc.sbuf_tensor("psum8", [8, 2], F32))
            pst = stA.enter_context(nc.psum_tensor("pstA", [128, 512], F32))
            xr = [stA.enter_context(nc.sbuf_tensor(f"xr{i}", [128, E], F32)) for i in range(2)]
            xnf = [stA.enter_context(nc.sbuf_tensor(f"xnf{i}", [128, E], F32)) for i in range(2)]
            xnb = [stA.enter_context(nc.sbuf_tensor(f"xnb{i}", [128, E], BF16)) for i in range(2)]

            xqb = [stA.enter_context(nc.sbuf_tensor(f"xqb{i}", [128, E], BF16)) for i in range(2)]
            sqscr = stA.enter_context(nc.sbuf_tensor("sqscr", [128, E], F32))
            wqi = stA.enter_context(nc.sbuf_tensor("wqi", [128, INNER], I32))
            xqi = wqi
            wqb = [stA.enter_context(nc.sbuf_tensor(f"wqb{i}", [128, INNER], BF16)) for i in range(2)]

            p.dma("sync", onesc[:], onc_in[:, :])
            ld_ones = p.dma("sync", onesr[:], onr_in[:, :])

            wv = {}
            for i in range(10):
                buf = wbuf[i % 2]
                w = [("vector", wv.get(i - 2, 0))]
                if i < 8:
                    ldw = p.dma("sync", buf[:, 0:E], w1_in[i * 128:(i + 1) * 128, :], w)
                    src, col = buf[:, 0:E], 0
                else:
                    ldw = p.dma("sync", buf[:], w2_in[(i - 8) * 128:(i - 7) * 128, :], w)
                    src, col = buf[:], 1
                if i in (0, 8):
                    v = p.emit("vector", (lambda E_, src=src, col=col: E_.tensor_reduce(
                        acc2[:, col:col + 1], src, axis=AX.X, op=ALU.add,
                        apply_absolute_value=True)), [("sync", ldw)])
                else:
                    v = p.emit("vector", (lambda E_, src=src: E_.tensor_reduce(
                        tiny[:, 26:27], src, axis=AX.X, op=ALU.add,
                        apply_absolute_value=True)), [("sync", ldw)])
                    v = p.emit("vector", (lambda E_, col=col: E_.tensor_tensor(
                        acc2[:, col:col + 1], acc2[:, col:col + 1], tiny[:, 26:27],
                        op=ALU.add)), [("vector", v)])
                wv[i] = v
            mm1 = p.emit("tensor", (lambda E_: E_.matmul(
                pst[0:1, 0:2], onesc[:], acc2[:], start=True, stop=True)),
                [("vector", wv[9]), ("sync", ld_ones)])
            sc1 = p.emit("scalar", (lambda E_: E_.copy(prow[:], pst[0:1, 0:2])), [("tensor", mm1)])
            st1 = p.dma("gpsimd", cc_s_in.ap()[:, :], prow[:], [("scalar", sc1)])
            ag1 = p.emit("gpsimd", (lambda E_: E_.collective_compute(
                "AllGather", ALU.bypass, replica_groups=[list(range(8))],
                ins=[cc_s_in.ap().opt()], outs=[cc_s_out.ap().opt()])), [("gpsimd", st1)])
            ld1 = p.dma("sync", psum8[:], cc_s_out.ap()[:, :], [("gpsimd", ag1)])
            mm2 = p.emit("tensor", (lambda E_: E_.matmul(
                pst[0:1, 4:6], onesc[0:8, :], psum8[:], start=True, stop=True,
                skip_group_check=True)), [("sync", ld1), ("scalar", sc1)])
            sc2 = p.emit("scalar", (lambda E_: E_.copy(tiny[0:1, 4:6], pst[0:1, 4:6])), [("tensor", mm2)])
            v = p.emit("vector", (lambda E_: E_.tensor_scalar(
                tiny[0:1, 4:6], tiny[0:1, 4:6], float(INVWN), float(EPS),
                op0=ALU.mult, op1=ALU.max)), [("scalar", sc2)])
            v = p.emit("vector", (lambda E_: E_.reciprocal(tiny[0:1, 6:8], tiny[0:1, 4:6])), [("vector", v)])
            mm3 = p.emit("tensor", (lambda E_: E_.matmul(
                pst[:, 8:12], onesr[:], tiny[0:1, 4:8], start=True, stop=True,
                skip_group_check=True)), [("vector", v), ("scalar", sc2)])
            scb = p.emit("scalar", (lambda E_: E_.copy(tiny[:, 2:4], pst[:, 8:10])), [("tensor", mm3)])
            scb = p.emit("scalar", (lambda E_: E_.copy(tiny[:, 0:2], pst[:, 10:12])), [("tensor", mm3)])
            ws_v, ws_s = v, scb

            xn_st, xq_st, vx = {}, {}, {}
            for rt in range(NRT):
                w = [("vector", vx[rt - 2])] if rt >= 2 else [("gpsimd", ag1)]
                ldx = p.dma("sync", xr[rt % 2][:], x_in[rt * 128:(rt + 1) * 128, :], w)
                wx = [("sync", ldx)]
                v = p.emit("vector", (lambda E_, rt=rt, t=xr[rt % 2]: E_.tensor_reduce(
                    stc(SUMS, rt), t[:], axis=AX.X, op=ALU.add)), wx)
                s = p.emit("scalar", (lambda E_, rt=rt, t=xr[rt % 2]: E_.activation(
                    sqscr[:, 0:E], t[:], AF.Square, accum_out=stc(SSQ, rt))), wx)
                v = p.emit("vector", (lambda E_, rt=rt: E_.tensor_scalar(
                    stc(MEAN, rt), stc(SUMS, rt), float(INV2048), None, op0=ALU.mult)), [("vector", v)])
                v = p.emit("vector", (lambda E_, rt=rt: E_.tensor_tensor(
                    tiny[:, 8:9], stc(MEAN, rt), stc(MEAN, rt), op=ALU.mult)), [("vector", v)])
                v = p.emit("vector", (lambda E_, rt=rt: E_.tensor_scalar(
                    tiny[:, 9:10], stc(SSQ, rt), float(INV2048), None, op0=ALU.mult)),
                    [("vector", v), ("scalar", s)])
                v = p.emit("vector", (lambda E_: E_.tensor_tensor(
                    tiny[:, 10:11], tiny[:, 9:10], tiny[:, 8:9], op=ALU.subtract)), [("vector", v)])
                v = p.emit("vector", (lambda E_: E_.tensor_scalar(
                    tiny[:, 10:11], tiny[:, 10:11], 1.0, float(EPS), op0=ALU.mult, op1=ALU.add)), [("vector", v)])
                s = p.emit("scalar", (lambda E_: E_.activation(
                    tiny[:, 11:12], tiny[:, 10:11], AF.Ln)), [("vector", v)])
                s = p.emit("scalar", (lambda E_: E_.activation(
                    tiny[:, 12:13], tiny[:, 11:12], AF.Exp, scale=-0.5)), [("scalar", s)])
                v = p.emit("vector", (lambda E_: E_.tensor_tensor(
                    tiny[:, 13:14], tiny[:, 12:13], tiny[:, 12:13], op=ALU.mult)), [("scalar", s)])
                v = p.emit("vector", (lambda E_: E_.tensor_tensor(
                    tiny[:, 13:14], tiny[:, 13:14], tiny[:, 10:11], op=ALU.mult)), [("vector", v)])
                v = p.emit("vector", (lambda E_: E_.tensor_scalar(
                    tiny[:, 13:14], tiny[:, 13:14], -0.5, 1.5, op0=ALU.mult, op1=ALU.add)), [("vector", v)])
                v = p.emit("vector", (lambda E_, rt=rt: E_.tensor_tensor(
                    stc(RSTD, rt), tiny[:, 12:13], tiny[:, 13:14], op=ALU.mult)), [("vector", v)])
                v = p.emit("vector", (lambda E_, rt=rt, t=xr[rt % 2], o=xnf[rt % 2]: E_.tensor_scalar(
                    o[:], t[:], stc(MEAN, rt), stc(RSTD, rt), op0=ALU.subtract, op1=ALU.mult)), [("vector", v)])
                vxn = v
                w2c_ = [("vector", v)]
                if rt >= 2:
                    w2c_.append(("gpsimd", xn_st[rt - 2]))
                v2 = p.emit("vector", (lambda E_, i=rt % 2: E_.tensor_copy(
                    xnb[i][:], xnf[i][:])), w2c_)
                xn_st[rt] = p.dma("gpsimd", cc_xn_in.ap()[rt * 128:(rt + 1) * 128, :],
                                  xnb[rt % 2][:],
                                  [("vector", v2)] + ([("gpsimd", ag1)] if rt == 0 else []))
                s = p.emit("scalar", (lambda E_, i=rt % 2: E_.activation(
                    sqscr[:, 0:E], xnf[i][:], AF.Square, accum_out=tiny[:, 14:15])), [("vector", vxn)])
                v = p.emit("vector", (lambda E_: E_.tensor_scalar(
                    tiny[:, 18:19], tiny[:, 14:15], float(INV2048), float(EPS), op0=ALU.mult, op1=ALU.add)),
                    [("scalar", s), ("vector", v2)])
                s = p.emit("scalar", (lambda E_: E_.activation(
                    tiny[:, 15:16], tiny[:, 18:19], AF.Ln)), [("vector", v)])
                s = p.emit("scalar", (lambda E_: E_.activation(
                    tiny[:, 16:17], tiny[:, 15:16], AF.Exp, scale=-0.5)), [("scalar", s)])
                v = p.emit("vector", (lambda E_: E_.tensor_tensor(
                    tiny[:, 17:18], tiny[:, 16:17], tiny[:, 16:17], op=ALU.mult)), [("scalar", s)])
                v = p.emit("vector", (lambda E_: E_.tensor_tensor(
                    tiny[:, 17:18], tiny[:, 17:18], tiny[:, 18:19], op=ALU.mult)), [("vector", v)])
                v = p.emit("vector", (lambda E_: E_.tensor_scalar(
                    tiny[:, 17:18], tiny[:, 17:18], -0.5, 1.5, op0=ALU.mult, op1=ALU.add)), [("vector", v)])
                v = p.emit("vector", (lambda E_: E_.tensor_tensor(
                    tiny[:, 19:20], tiny[:, 16:17], tiny[:, 17:18], op=ALU.mult)), [("vector", v)])
                v = p.emit("vector", (lambda E_, i=rt % 2: E_.tensor_reduce(
                    tiny[:, 20:21], xnf[i][:], axis=AX.X, op=ALU.max,
                    apply_absolute_value=True)), [("vector", v)])
                v = p.emit("vector", (lambda E_: E_.tensor_tensor(
                    tiny[:, 21:22], tiny[:, 19:20], tiny[:, 20:21], op=ALU.mult)), [("vector", v)])
                v = p.emit("vector", (lambda E_: E_.tensor_scalar(
                    tiny[:, 21:22], tiny[:, 21:22], float(EPS), None, op0=ALU.max)), [("vector", v)])
                v = p.emit("vector", (lambda E_: E_.reciprocal(tiny[:, 22:23], tiny[:, 21:22])), [("vector", v)])
                v = p.emit("vector", (lambda E_: E_.tensor_scalar(
                    tiny[:, 23:24], tiny[:, 22:23], 127.0, None, op0=ALU.mult)), [("vector", v)])
                v = p.emit("vector", (lambda E_: E_.tensor_tensor(
                    tiny[:, 24:25], tiny[:, 19:20], tiny[:, 23:24], op=ALU.mult)), [("vector", v)])
                v = p.emit("vector", (lambda E_, rt=rt: E_.tensor_scalar(
                    stc(EV1, rt), tiny[:, 21:22], float(INV127), tiny[:, 2:3],
                    op0=ALU.mult, op1=ALU.mult)), [("vector", v), ("vector", ws_v), ("scalar", ws_s)])
                v = p.emit("vector", (lambda E_, i=rt % 2: E_.tensor_scalar(
                    xqi[:, 0:E], xnf[i][:], tiny[:, 24:25], None, op0=ALU.mult)), [("vector", v)])
                wq_ = [("vector", v)]
                if rt >= 2:
                    wq_.append(("gpsimd", xq_st[rt - 2]))
                v = p.emit("vector", (lambda E_, i=rt % 2: E_.tensor_copy(
                    xqb[i][:], xqi[:, 0:E])), wq_)
                xq_st[rt] = p.dma("gpsimd", xq_scr.ap()[rt * 128:(rt + 1) * 128, :],
                                  xqb[rt % 2][:], [("vector", v)])
                vx[rt] = v

            ag2a = p.emit("gpsimd", (lambda E_: E_.collective_compute(
                "AllGather", ALU.bypass, replica_groups=[[0, 1, 2, 3], [4, 5, 6, 7]],
                ins=[cc_xn_in.ap().opt()], outs=[cc_xn_out.ap().opt()])),
                [("gpsimd", xn_st[NRT - 1])])

            wq_st = 0
            qv = {}
            for i in range(10):
                buf = wbuf[i % 2]
                w = [("vector", qv[i - 2])] if i >= 2 else [("vector", wv[9]), ("gpsimd", ag2a)]
                if i < 8:
                    ldw = p.dma("sync", buf[:, 0:E], w1_in[i * 128:(i + 1) * 128, :], w)
                    src, n, col = buf[:, 0:E], E, 0
                else:
                    ldw = p.dma("sync", buf[:], w2_in[(i - 8) * 128:(i - 7) * 128, :], w)
                    src, n, col = buf[:], INNER, 1
                v = p.emit("vector", (lambda E_, src=src, n=n, col=col: E_.tensor_scalar(
                    wqi[:, 0:n], src, tiny[:, col:col + 1], None, op0=ALU.mult)),
                    [("sync", ldw), ("vector", ws_v), ("scalar", ws_s)])
                v = p.emit("vector", (lambda E_, n=n: E_.tensor_scalar(
                    wqi[:, 0:n], wqi[:, 0:n], -1, 1, op0=ALU.max, op1=ALU.min)), [("vector", v)])
                wb_ = [("vector", v)]
                if i >= 2:
                    wb_.append(("gpsimd", wq_sts[i - 2]))
                v = p.emit("vector", (lambda E_, n=n, i=i: E_.tensor_copy(
                    wqb[i % 2][:, 0:n], wqi[:, 0:n])), wb_)
                qv[i] = v
                if i == 0:
                    wq_sts = {}
                dst = cw_in_w1[i * 128:(i + 1) * 128, :] if i < 8 else cw_in_w2[(i - 8) * 128:(i - 7) * 128, :]
                wq_sts[i] = p.dma("gpsimd", dst, wqb[i % 2][:, 0:n],
                                  [("vector", v)] + ([("gpsimd", ag2a)] if i == 0 else []))
            agw = [("gpsimd", wq_sts[9])]
            ag2b1 = p.emit("gpsimd", (lambda E_: E_.collective_compute(
                "AllGather", ALU.bypass, replica_groups=[list(range(8))],
                ins=[cc_w1_in.ap().opt()], outs=[cc_w1_out.ap().opt()])), agw)
            ag2b = p.emit("gpsimd", (lambda E_: E_.collective_compute(
                "AllGather", ALU.bypass, replica_groups=[list(range(8))],
                ins=[cc_w2_in.ap().opt()], outs=[cc_w2_out.ap().opt()])), [("gpsimd", ag2b1)])
            p.emit("gpsimd", (lambda E_: None), [("gpsimd", ag2b)], inc=0)
            if os.environ.get("MK_DBG") == "endA":
                p.dma("gpsimd", dbg_out[:, :], cc_w1_out.ap()[0:2048, :], [("gpsimd", ag2b)])
            if os.environ.get("MK_DBG") == "endA2":
                p.dma("gpsimd", dbg_out[:, :], cc_w2_out.ap()[0:512, 0:8192].rearrange("a (c b) -> (a c) b", b=2048),
                      [("gpsimd", ag2b)])

            p.barrier_waits()
            p.flush(nc, sems)
        if int(os.environ.get("MK_PHASES", "9")) <= 1:
            return nc

        stBC = contextlib.ExitStack()
        with stBC:
            wqs = stBC.enter_context(nc.sbuf_tensor("wqs", [128, 16 * 256], BF16))
            wks = stBC.enter_context(nc.sbuf_tensor("wks", [128, 16 * 256], BF16))
            wvs = stBC.enter_context(nc.sbuf_tensor("wvs", [128, 16 * 256], BF16))
            xnt = [stBC.enter_context(nc.sbuf_tensor(f"xnt{i}", [128, 16 * 512], BF16)) for i in range(2)]
            qt = [stBC.enter_context(nc.sbuf_tensor(f"qt{i}", [128, S], BF16)) for i in range(2)]
            kt = [stBC.enter_context(nc.sbuf_tensor(f"kt{i}", [128, S], BF16)) for i in range(2)]
            vt = [stBC.enter_context(nc.sbuf_tensor(f"vt{i}", [128, S], BF16)) for i in range(2)]
            btab = stBC.enter_context(nc.sbuf_tensor("btab", [128, 256], F32))
            msks = stBC.enter_context(nc.sbuf_tensor("msks", [128, 4 * QB], BF16))
            expt = [stBC.enter_context(nc.sbuf_tensor(f"expt{i}", [128, 3 * QB], BF16)) for i in range(2)]
            dacc = [stBC.enter_context(nc.sbuf_tensor(f"dacc{i}", [128, QB], F32)) for i in range(2)]
            dens = [stBC.enter_context(nc.sbuf_tensor(f"dens{i}", [1, QB], F32)) for i in range(2)]
            recs = [stBC.enter_context(nc.sbuf_tensor(f"recs{i}", [1, QB], F32)) for i in range(2)]
            recb = [stBC.enter_context(nc.sbuf_tensor(f"recb{i}", [128, QB], F32)) for i in range(2)]
            ctxs = [stBC.enter_context(nc.sbuf_tensor(f"ctxs{i}", [128, 2 * QB], BF16)) for i in range(2)]
            w2tb = [stBC.enter_context(nc.sbuf_tensor(f"w2tb{i}", [128, E], BF16)) for i in range(2)]
            ps_s = [stBC.enter_context(nc.psum_tensor(f"ps_s{i}", [128, QB], F32)) for i in range(4)]
            ps_c = [stBC.enter_context(nc.psum_tensor(f"ps_c{i}", [128, QB], F32)) for i in range(2)]
            ps_d = stBC.enter_context(nc.psum_tensor("ps_d", [1, QB], F32))
            ps_rb = stBC.enter_context(nc.psum_tensor("ps_rb", [128, QB], F32))

            p.dma("sync", btab[:], btab_in[:, :])
            p.dma("sync", msks[:], mask_in[:, :])
            if os.environ.get("MK_DBG") == "early":
                p.dma("gpsimd", dbg_out[:, :],
                      cc_w1_out.ap()[0:2048, :], [("gpsimd", ag2b)])
            ctx_st = {}
            ctx_last = {0: 0, 1: 0}
            w2t_st = {}
            qkv_mm_last = 0
            for half in range(2):
                hsl = [half * 2, half * 2 + 1]
                wld = 0
                for (dst, srcw) in ((wqs, wq_in), (wks, wk_in), (wvs, wv_in)):
                    wld = p.dma("gpsimd", dst[:].rearrange("p (c w) -> p c w", w=256),
                                srcw[:, half * 256:(half + 1) * 256].rearrange("(c p) w -> p c w", p=128),
                                ([("tensor", qkv_mm_last)] if half else []) + [("gpsimd", ag2b)])
                evq = {}
                pcev = {0: 0, 1: 0}
                attn_v = p.counts["vector"]
                attn_s = p.counts["scalar"]
                for rb in range(16):
                    w = [("gpsimd", ag2a), ("gpsimd", wld)]
                    if rb >= 2:
                        w.append(("vector", evq[rb - 2][0]))
                        w.append(("scalar", evq[rb - 2][1]))
                    ldt = 0
                    for dc in range(16):
                        ldt = p.dma("sync", xnt[rb % 2][:, dc * 512:(dc + 1) * 512],
                                    cc_xn_out.ap()[rb * 512:(rb + 1) * 512, dc * 128:(dc + 1) * 128],
                                    w if dc == 0 else (), transpose=True)
                    wl = [("sync", ldt)]
                    if rb >= 1:
                        wl += [("vector", evq[rb - 1][0]), ("scalar", evq[rb - 1][1])]
                    if half == 1 and rb < 2:
                        wl += [("vector", attn_v), ("scalar", attn_s)]
                    mmq = {}
                    for qk, wmat in ((0, wqs), (1, wks)):
                        for ih in range(2):
                            pb = ps_s[qk * 2 + ih]
                            wcol = ih * 128
                            for dc in range(16):
                                mmq[(qk, ih)] = p.emit("tensor", (lambda E_, pb=pb, wmat=wmat, dc=dc, wcol=wcol, i=rb % 2: E_.matmul(
                                    pb[:], wmat[:, dc * 256 + wcol:dc * 256 + wcol + 128],
                                    xnt[i][:, dc * 512:(dc + 1) * 512],
                                    start=(dc == 0), stop=(dc == 15), skip_group_check=True)),
                                    wl if dc == 0 else ())
                    mmv = {}
                    ev_v = 0
                    for rs4 in range(4):
                        pb = ps_c[rs4 % 2]
                        wv_ = list(wl) + ([("vector", pcev[rs4 % 2])] if pcev[rs4 % 2] else [])
                        for dc in range(16):
                            mmv[rs4] = p.emit("tensor", (lambda E_, pb=pb, dc=dc, rs4=rs4, i=rb % 2: E_.matmul(
                                pb[:, 0:256], xnt[i][:, dc * 512 + rs4 * 128:dc * 512 + rs4 * 128 + 128],
                                wvs[:, dc * 256:dc * 256 + 256],
                                start=(dc == 0), stop=(dc == 15), skip_group_check=True)),
                                wv_ if dc == 0 else ())
                        kc = rb * 4 + rs4
                        for ih in range(2):
                            ev_v = p.emit("vector", (lambda E_, ih=ih, kc=kc, rs4=rs4: E_.tensor_copy(
                                vt[ih][:, kc * 128:(kc + 1) * 128],
                                ps_c[rs4 % 2][:, ih * 128:(ih + 1) * 128])), [("tensor", mmv[rs4])])
                        pcev[rs4 % 2] = ev_v
                    ev_s = 0
                    for ih in range(2):
                        ev_s = p.emit("scalar", (lambda E_, ih=ih, rb=rb: E_.copy(
                            qt[ih][:, rb * 512:(rb + 1) * 512], ps_s[ih][:])), [("tensor", mmq[(0, ih)])])
                        ev_s = p.emit("scalar", (lambda E_, ih=ih, rb=rb: E_.copy(
                            kt[ih][:, rb * 512:(rb + 1) * 512], ps_s[2 + ih][:])), [("tensor", mmq[(1, ih)])])
                    evq[rb] = (ev_v, ev_s)
                    qkv_mm_last = mmv[3]
                lastq_v, lastq_s = evq[15]

                prev_ev = {0: ("vector", lastq_v), 1: ("vector", lastq_v)}
                exp_cnt, rdy, acc_cnt, ctx_mm = {}, {}, {}, {}
                psd_rd = 0
                psrb_rd = 0
                first_sc = [("vector", lastq_v), ("scalar", lastq_s)]
                for j in range(NJ):
                    nt = 4 * j + 4

                    def emit_scores(t, ih, j=j, nt=nt):
                        ps = ps_s[ih * 2 + (t % 2)]
                        wts = list(first_sc)
                        if (ih, t - 2) in exp_cnt:
                            wts.append(("scalar", exp_cnt[(ih, t - 2)]))
                        smm = p.emit("tensor", (lambda E_, ps=ps, t=t, j=j, ih=ih: E_.matmul(
                            ps[:], kt[ih][:, t * 128:(t + 1) * 128],
                            qt[ih][:, j * QB:(j + 1) * QB],
                            start=True, stop=True, skip_group_check=True)), wts)
                        off = hsl[ih] * 64 + (t - 4 * j + 60)
                        we = [("tensor", smm)]
                        if (ih, t - 3) in ctx_mm:
                            we.append(("tensor", ctx_mm[(ih, t - 3)]))
                            we.append(("vector", acc_cnt[(ih, t - 3)]))
                        ec = p.emit("scalar", (lambda E_, ps=ps, t=t, off=off, ih=ih: E_.activation(
                            expt[ih][:, (t % 3) * QB:(t % 3 + 1) * QB], ps[:],
                            AF.Exp, bias=btab[:, off:off + 1], scale=INVSQD)), we)
                        exp_cnt[(ih, t)] = ec
                        dep = ("scalar", ec)
                        if t >= 4 * j:
                            rel = t - 4 * j
                            mc = p.emit("vector", (lambda E_, t=t, rel=rel, ih=ih: E_.tensor_tensor(
                                expt[ih][:, (t % 3) * QB:(t % 3 + 1) * QB],
                                expt[ih][:, (t % 3) * QB:(t % 3 + 1) * QB],
                                msks[:, rel * QB:(rel + 1) * QB], op=ALU.mult)), [dep])
                            dep = ("vector", mc)
                        rdy[(ih, t)] = dep

                    def emit_body(t, ih, j=j, nt=nt):
                        wv_ = [rdy[(ih, t)]]
                        if t == 0:
                            wv_.append(prev_ev[ih])
                            ac = p.emit("vector", (lambda E_, t=t, ih=ih: E_.tensor_copy(
                                dacc[ih][:], expt[ih][:, (t % 3) * QB:(t % 3 + 1) * QB])), wv_)
                        else:
                            ac = p.emit("vector", (lambda E_, t=t, ih=ih: E_.tensor_tensor(
                                dacc[ih][:], dacc[ih][:],
                                expt[ih][:, (t % 3) * QB:(t % 3 + 1) * QB], op=ALU.add)), wv_)
                        acc_cnt[(ih, t)] = ac
                        wt = [rdy[(ih, t)]]
                        if t == 0:
                            wt.append(prev_ev[ih])
                        ctx_mm[(ih, t)] = p.emit("tensor", (lambda E_, t=t, nt=nt, ih=ih: E_.matmul(
                            ps_c[ih][:], vt[ih][:, t * 128:(t + 1) * 128],
                            expt[ih][:, (t % 3) * QB:(t % 3 + 1) * QB],
                            start=(t == 0), stop=(t == nt - 1), skip_group_check=True)), wt)

                    for ih in range(2):
                        emit_scores(0, ih)
                    for t in range(nt):
                        for ih in range(2):
                            if t + 1 < nt:
                                emit_scores(t + 1, ih)
                            emit_body(t, ih)
                    for ih in range(2):
                        wd = [("vector", acc_cnt[(ih, nt - 1)])]
                        if psd_rd:
                            wd.append(("scalar", psd_rd))
                        mm = p.emit("tensor", (lambda E_, ih=ih: E_.matmul(
                            ps_d[:], onesc[:], dacc[ih][:], start=True, stop=True,
                            skip_group_check=True)), wd)
                        sc = p.emit("scalar", (lambda E_, ih=ih: E_.copy(dens[ih][:], ps_d[:])), [("tensor", mm)])
                        psd_rd = sc
                        rc = p.emit("vector", (lambda E_, ih=ih: E_.reciprocal(recs[ih][:], dens[ih][:])), [("scalar", sc)])
                        wb_ = [("vector", rc)]
                        if psrb_rd:
                            wb_.append(("scalar", psrb_rd))
                        mm = p.emit("tensor", (lambda E_, ih=ih: E_.matmul(
                            ps_rb[:], onesr[:], recs[ih][:], start=True, stop=True,
                            skip_group_check=True)), wb_)
                        sc = p.emit("scalar", (lambda E_, ih=ih: E_.copy(recb[ih][:], ps_rb[:])), [("tensor", mm)])
                        psrb_rd = sc
                        wts = [("scalar", sc), ("tensor", ctx_mm[(ih, nt - 1)])]
                        if half == 0 and j == 0:
                            wts.append(("gpsimd", ag2b))
                        if (half, ih, j - 2) in ctx_st:
                            wts.append(("gpsimd", ctx_st[(half, ih, j - 2)]))
                        if half == 1 and j < 2 and ctx_last[ih]:
                            wts.append(("gpsimd", ctx_last[ih]))
                        ev = p.emit("vector", (lambda E_, ih=ih, j=j: E_.tensor_tensor(
                            ctxs[ih][:, (j % 2) * QB:(j % 2 + 1) * QB], ps_c[ih][:],
                            recb[ih][:], op=ALU.mult)), wts)
                        prev_ev[ih] = ("vector", ev)
                        hloc = hsl[ih]
                        ctx_st[(half, ih, j)] = p.dma(
                            "gpsimd",
                            ctx_scr.ap()[hloc * 128:hloc * 128 + 128, j * QB:(j + 1) * QB],
                            ctxs[ih][:, (j % 2) * QB:(j % 2 + 1) * QB], [("vector", ev)])
                if half == 0:
                    ctx_last = {ih: ctx_st[(0, ih, NJ - 1)] for ih in range(2)}
                    for oc in range(64):
                        wpre = [("gpsimd", ag2b)]
                        if oc >= 2:
                            wpre.append(("gpsimd", w2t_st[oc - 2]))
                        ld = 0
                        for rk in range(8):
                            ld = p.dma("sync", w2tb[oc % 2][:, rk * 256:(rk + 1) * 256],
                                       w2q_view(rk)[:, oc * 128:(oc + 1) * 128],
                                       wpre if rk == 0 else (), transpose=True)
                        w2t_st[oc] = p.dma("gpsimd", w2t_scr.ap()[oc * 128:(oc + 1) * 128, :],
                                           w2tb[oc % 2][:], [("sync", ld)])
            ctx_all = ctx_st[(1, 1, NJ - 1)]
            w2t_done = w2t_st[63]
            if os.environ.get("MK_DBG") == "endBC":
                p.dma("gpsimd", dbg_out[:, :],
                      cc_w1_out.ap()[0:2048, :], [("gpsimd", ctx_all), ("gpsimd", w2t_done)])
            if os.environ.get("MK_DBG") == "endBC2":
                p.dma("gpsimd", dbg_out[:, :],
                      w2t_scr.ap()[0:2048, :], [("gpsimd", ctx_all), ("gpsimd", w2t_done)])
            p.barrier_waits()
            p.flush(nc, sems)
        if int(os.environ.get("MK_PHASES", "9")) <= 2:
            return nc

        stD = contextlib.ExitStack()
        with stD:
            xqt = stD.enter_context(nc.sbuf_tensor("xqt", [128, 16 * 2048], BF16))
            w1t = [stD.enter_context(nc.sbuf_tensor(f"w1t{i}", [128, 16 * 512], BF16)) for i in range(2)]
            b1bc = stD.enter_context(nc.sbuf_tensor("b1bc", [128, INNER], BF16))
            brow8 = stD.enter_context(nc.sbuf_tensor("brow8", [1, INNER], F32))
            ps_bc = stD.enter_context(nc.psum_tensor("ps_bcD", [128, 512], F32))
            y1b = [stD.enter_context(nc.sbuf_tensor(f"y1b{i}", [128, 512], F32)) for i in range(2)]
            sgt = [stD.enter_context(nc.sbuf_tensor(f"sgt{i}", [128, 512], F32)) for i in range(2)]
            ut = [stD.enter_context(nc.sbuf_tensor(f"ut{i}", [128, 512], F32)) for i in range(2)]
            egt = [stD.enter_context(nc.sbuf_tensor(f"egt{i}", [128, 512], F32)) for i in range(2)]
            h2t = [stD.enter_context(nc.sbuf_tensor(f"h2t{i}", [128, 512], F32)) for i in range(2)]
            at = [stD.enter_context(nc.sbuf_tensor(f"at{i}", [128, 512], BF16)) for i in range(2)]
            sqs2 = stD.enter_context(nc.sbuf_tensor("sqs2", [128, 512], F32))
            ps_y = [stD.enter_context(nc.psum_tensor(f"ps_y{i}", [128, 512], F32)) for i in range(2)]

            ldq = 0
            for dc in range(16):
                ldq = p.dma("sync", xqt[:, dc * 2048:(dc + 1) * 2048],
                            xq_scr.ap()[:, dc * 128:(dc + 1) * 128],
                            [("gpsimd", xq_st[NRT - 1])] if dc == 0 else (), transpose=True)
            ldbr = p.dma("sync", brow8[:], b1_in[:, :])
            ldb = 0
            scb2 = 0
            for bc_i in range(16):
                wmb = [("sync", ldbr)]
                if scb2:
                    wmb.append(("scalar", scb2))
                mmb = p.emit("tensor", (lambda E_, bc_i=bc_i: E_.matmul(
                    ps_bc[:], onesr[:], brow8[0:1, bc_i * 512:(bc_i + 1) * 512],
                    start=True, stop=True, skip_group_check=True)), wmb)
                scb2 = p.emit("scalar", (lambda E_, bc_i=bc_i: E_.copy(
                    b1bc[:, bc_i * 512:(bc_i + 1) * 512], ps_bc[:])), [("tensor", mmb)])
            ldb = scb2
            vch, a_st, mm_last = {}, {}, {}
            s = 0
            for oc in range(16):
                wdep = [("gpsimd", ag2b)]
                if oc >= 2:
                    wdep.append(("tensor", mm_last[(oc - 2, NRT - 1)]))
                ldw = 0
                rank = oc // 2
                for dc in range(16):
                    ldw = p.dma("sync", w1t[oc % 2][:, dc * 512:(dc + 1) * 512],
                                w1q_view(rank)[(oc % 2) * 512:(oc % 2) * 512 + 512, dc * 128:(dc + 1) * 128],
                                wdep if dc == 0 else (), transpose=True)
                for rt in range(NRT):
                    i = rt % 2
                    wts = [("sync", ldw), ("sync", ldq)]
                    prv = (oc, rt - 2) if rt >= 2 else (oc - 1, rt + 14)
                    if prv in vch:
                        wts.append(("vector", vch[prv]))
                    mm = 0
                    for dc in range(16):
                        mm = p.emit("tensor", (lambda E_, i=i, dc=dc, oc=oc, rt=rt: E_.matmul(
                            ps_y[i][:], xqt[:, dc * 2048 + rt * 128:dc * 2048 + rt * 128 + 128],
                            w1t[oc % 2][:, dc * 512:(dc + 1) * 512],
                            start=(dc == 0), stop=(dc == 15), skip_group_check=True)),
                            wts if dc == 0 else ())
                    mm_last[(oc, rt)] = mm
                    v = p.emit("vector", (lambda E_, i=i, rt=rt, oc=oc: E_.scalar_tensor_tensor(
                        y1b[i][:], ps_y[i][:], stc(EV1, rt), b1bc[:, oc * 512:(oc + 1) * 512],
                        op0=ALU.mult, op1=ALU.add)), [("tensor", mm), ("scalar", ldb)])
                    s = p.emit("scalar", (lambda E_, i=i: E_.activation(
                        sgt[i][:], y1b[i][:], AF.Tanh, scale=0.5)), [("vector", v)])
                    v = p.emit("vector", (lambda E_, i=i: E_.scalar_tensor_tensor(
                        ut[i][:], sgt[i][:], 1.0, y1b[i][:], op0=ALU.add, op1=ALU.mult)), [("scalar", s)])
                    s = p.emit("scalar", (lambda E_, i=i: E_.activation(
                        h2t[i][:], ut[i][:], AF.Gelu, scale=0.5)), [("vector", v)])
                    wa = [("scalar", s)]
                    if prv in a_st:
                        wa.append(("gpsimd", a_st[prv]))
                    v = p.emit("vector", (lambda E_, i=i, rt=rt, oc=oc: E_.tensor_scalar(
                        at[i][:], h2t[i][:], 1.0, 0.0, op0=ALU.mult, op1=ALU.add,
                        accum_out=fstats[:, rt * 16 + oc:rt * 16 + oc + 1])), wa)
                    s = p.emit("scalar", (lambda E_, i=i, rt=rt, oc=oc: E_.activation(
                        sqs2[:], h2t[i][:], AF.Square,
                        accum_out=fstats[:, 256 + rt * 16 + oc:256 + rt * 16 + oc + 1])), [("vector", v)])
                    vch[(oc, rt)] = v
                    a_st[(oc, rt)] = p.dma("gpsimd",
                                           a_spill.ap()[rt * 128:(rt + 1) * 128, oc * 512:(oc + 1) * 512],
                                           at[i][:], [("vector", v)])
            fc1_s_last = s
            if os.environ.get("MK_DBG") == "endD":
                p.dma("gpsimd", dbg_out[:, :],
                      cc_w1_out.ap()[0:2048, :], [("gpsimd", a_st[(15, 15)]), ("scalar", s)])
            p.barrier_waits()
            p.flush(nc, sems)
        if int(os.environ.get("MK_PHASES", "9")) <= 4:
            return nc

        stE = contextlib.ExitStack()
        with stE:
            gbc = stE.enter_context(nc.sbuf_tensor("gbc", [128, INNER], BF16))
            bbc = stE.enter_context(nc.sbuf_tensor("bbc", [128, INNER], BF16))
            a2 = [stE.enter_context(nc.sbuf_tensor(f"a2_{i}", [128, 4096], BF16)) for i in range(2)]
            hrow = stE.enter_context(nc.sbuf_tensor("hrow", [128, INNER], F32))
            hqi = stE.enter_context(nc.sbuf_tensor("hqi", [128, 4096], I32))
            hqb = stE.enter_context(nc.sbuf_tensor("hqb", [128, INNER], BF16))
            hqt = stE.enter_context(nc.sbuf_tensor("hqt", [128, 64 * 512], BF16))
            w2c = [stE.enter_context(nc.sbuf_tensor(f"w2c{i}", [128, 1024], BF16)) for i in range(2)]
            ffo = stE.enter_context(nc.sbuf_tensor("ffo", [128, 1024], F32))
            sq2b = stE.enter_context(nc.sbuf_tensor("sq2b", [128, 4096], F32))

            with nc.psum_tensor("ps_bcE", [128, 512], F32) as ps_bcE:
                ldg1 = p.dma("sync", hrow[0:1, :], g_in[:, :])
                scb3 = 0
                lastmm = 0
                for bc_i in range(16):
                    wmb = [("sync", ldg1)]
                    if scb3:
                        wmb.append(("scalar", scb3))
                    lastmm = p.emit("tensor", (lambda E_, bc_i=bc_i: E_.matmul(
                        ps_bcE[:], onesr[:], hrow[0:1, bc_i * 512:(bc_i + 1) * 512],
                        start=True, stop=True, skip_group_check=True)), wmb)
                    scb3 = p.emit("scalar", (lambda E_, bc_i=bc_i: E_.copy(
                        gbc[:, bc_i * 512:(bc_i + 1) * 512], ps_bcE[:])), [("tensor", lastmm)])
                ldg2 = p.dma("sync", hrow[0:1, :], bln_in[:, :], [("tensor", lastmm)])
                for bc_i in range(16):
                    wmb = [("sync", ldg2), ("scalar", scb3)]
                    lastmm = p.emit("tensor", (lambda E_, bc_i=bc_i: E_.matmul(
                        ps_bcE[:], onesr[:], hrow[0:1, bc_i * 512:(bc_i + 1) * 512],
                        start=True, stop=True, skip_group_check=True)), wmb)
                    scb3 = p.emit("scalar", (lambda E_, bc_i=bc_i: E_.copy(
                        bbc[:, bc_i * 512:(bc_i + 1) * 512], ps_bcE[:])), [("tensor", lastmm)])
                ldg = scb3
            ps_y2 = [stE.enter_context(nc.psum_tensor(f"ps_y2{i}", [128, 1024], F32)) for i in range(4)]
            v = 0
            for rt in range(NRT):
                v = p.emit("vector", (lambda E_, rt=rt: E_.tensor_reduce(
                    stc(MU2, rt), fstats[:, rt * 16:(rt + 1) * 16], axis=AX.X, op=ALU.add)),
                    [("scalar", fc1_s_last)] if rt == 0 else [])
                v = p.emit("vector", (lambda E_, rt=rt: E_.tensor_reduce(
                    tiny[:, 8:9], fstats[:, 256 + rt * 16:256 + (rt + 1) * 16],
                    axis=AX.X, op=ALU.add)), [])
                v = p.emit("vector", (lambda E_, rt=rt: E_.tensor_scalar(
                    stc(MU2, rt), stc(MU2, rt), float(INV8192), None, op0=ALU.mult)), [])
                v = p.emit("vector", (lambda E_, rt=rt: E_.tensor_tensor(
                    tiny[:, 9:10], stc(MU2, rt), stc(MU2, rt), op=ALU.mult)), [])
                v = p.emit("vector", (lambda E_, rt=rt: E_.tensor_scalar(
                    tiny[:, 8:9], tiny[:, 8:9], float(INV8192), None, op0=ALU.mult)), [])
                v = p.emit("vector", (lambda E_, rt=rt: E_.tensor_tensor(
                    ev2[:, rt:rt + 1], tiny[:, 8:9], tiny[:, 9:10], op=ALU.subtract)), [])
                v = p.emit("vector", (lambda E_, rt=rt: E_.tensor_scalar(
                    ev2[:, rt:rt + 1], ev2[:, rt:rt + 1], 1.0, float(EPS),
                    op0=ALU.mult, op1=ALU.add)), [])
            s = p.emit("scalar", (lambda E_: E_.activation(
                rsq2[:], ev2[:], AF.Ln)), [("vector", v)])
            s = p.emit("scalar", (lambda E_: E_.activation(
                rsq2[:], rsq2[:], AF.Exp, scale=-0.5)), [("scalar", s)])
            v = p.emit("vector", (lambda E_: E_.tensor_tensor(
                tiny[:, 8:24], rsq2[:], rsq2[:], op=ALU.mult)), [("scalar", s)])
            v = p.emit("vector", (lambda E_: E_.tensor_tensor(
                tiny[:, 8:24], tiny[:, 8:24], ev2[:], op=ALU.mult)), [("vector", v)])
            v = p.emit("vector", (lambda E_: E_.tensor_scalar(
                tiny[:, 8:24], tiny[:, 8:24], -0.5, 1.5, op0=ALU.mult, op1=ALU.add)), [("vector", v)])
            vstat = p.emit("vector", (lambda E_: E_.tensor_tensor(
                sts(RSTD2), rsq2[:], tiny[:, 8:24], op=ALU.mult)), [("vector", v)])


            hq_st, p2v = {}, {}
            for rt in range(NRT):
                deps = [("vector", vstat), ("scalar", ldg), ("gpsimd", a_st[(15, rt)])]
                if rt:
                    deps.append(("vector", p2v[rt - 1]))
                for hf in range(2):
                    lda = p.dma("sync", a2[hf][:],
                                a_spill.ap()[rt * 128:(rt + 1) * 128, hf * 4096:(hf + 1) * 4096],
                                deps if hf == 0 else ())
                    v = p.emit("vector", (lambda E_, hf=hf, rt=rt: E_.scalar_tensor_tensor(
                        hrow[:, hf * 4096:(hf + 1) * 4096], a2[hf][:], stc(MU2, rt),
                        gbc[:, hf * 4096:(hf + 1) * 4096], op0=ALU.subtract, op1=ALU.mult)),
                        [("sync", lda)])
                    v = p.emit("vector", (lambda E_, hf=hf, rt=rt: E_.scalar_tensor_tensor(
                        hrow[:, hf * 4096:(hf + 1) * 4096], hrow[:, hf * 4096:(hf + 1) * 4096],
                        stc(RSTD2, rt), bbc[:, hf * 4096:(hf + 1) * 4096],
                        op0=ALU.mult, op1=ALU.add)), [("vector", v)])
                s = p.emit("scalar", (lambda E_: E_.activation(
                    sq2b[:], hrow[:, 0:4096], AF.Square, accum_out=tiny[:, 8:9])), [("vector", v)])
                s = p.emit("scalar", (lambda E_: E_.activation(
                    sq2b[:], hrow[:, 4096:8192], AF.Square, accum_out=tiny[:, 9:10])), [("scalar", s)])
                v = p.emit("vector", (lambda E_: E_.tensor_tensor(
                    tiny[:, 10:11], tiny[:, 8:9], tiny[:, 9:10], op=ALU.add)), [("scalar", s), ("vector", v)])
                v = p.emit("vector", (lambda E_: E_.tensor_scalar(
                    tiny[:, 10:11], tiny[:, 10:11], float(INV8192), float(EPS), op0=ALU.mult, op1=ALU.add)), [("vector", v)])
                s = p.emit("scalar", (lambda E_: E_.activation(
                    tiny[:, 11:12], tiny[:, 10:11], AF.Ln)), [("vector", v)])
                s = p.emit("scalar", (lambda E_: E_.activation(
                    tiny[:, 11:12], tiny[:, 11:12], AF.Exp, scale=-0.5)), [("scalar", s)])
                v = p.emit("vector", (lambda E_: E_.tensor_tensor(
                    tiny[:, 12:13], tiny[:, 11:12], tiny[:, 11:12], op=ALU.mult)), [("scalar", s)])
                v = p.emit("vector", (lambda E_: E_.tensor_tensor(
                    tiny[:, 12:13], tiny[:, 12:13], tiny[:, 10:11], op=ALU.mult)), [("vector", v)])
                v = p.emit("vector", (lambda E_: E_.tensor_scalar(
                    tiny[:, 12:13], tiny[:, 12:13], -0.5, 1.5, op0=ALU.mult, op1=ALU.add)), [("vector", v)])
                v = p.emit("vector", (lambda E_: E_.tensor_tensor(
                    tiny[:, 13:14], tiny[:, 11:12], tiny[:, 12:13], op=ALU.mult)), [("vector", v)])
                v = p.emit("vector", (lambda E_: E_.tensor_reduce(
                    tiny[:, 14:15], hrow[:], axis=AX.X, op=ALU.max, apply_absolute_value=True)), [("vector", v)])
                v = p.emit("vector", (lambda E_: E_.tensor_tensor(
                    tiny[:, 15:16], tiny[:, 13:14], tiny[:, 14:15], op=ALU.mult)), [("vector", v)])
                v = p.emit("vector", (lambda E_: E_.tensor_scalar(
                    tiny[:, 15:16], tiny[:, 15:16], float(EPS), None, op0=ALU.max)), [("vector", v)])
                v = p.emit("vector", (lambda E_: E_.reciprocal(tiny[:, 16:17], tiny[:, 15:16])), [("vector", v)])
                v = p.emit("vector", (lambda E_: E_.tensor_scalar(
                    tiny[:, 17:18], tiny[:, 16:17], 127.0, None, op0=ALU.mult)), [("vector", v)])
                v = p.emit("vector", (lambda E_, rt=rt: E_.tensor_tensor(
                    rsq2[:, rt:rt + 1], tiny[:, 13:14], tiny[:, 17:18], op=ALU.mult)), [("vector", v)])
                v = p.emit("vector", (lambda E_, rt=rt: E_.tensor_scalar(
                    ev2[:, rt:rt + 1], tiny[:, 15:16], float(INV127), tiny[:, 3:4],
                    op0=ALU.mult, op1=ALU.mult)), [("vector", v)])
                for hf in range(2):
                    v = p.emit("vector", (lambda E_, hf=hf, rt=rt: E_.tensor_scalar(
                        hqi[:], hrow[:, hf * 4096:(hf + 1) * 4096], rsq2[:, rt:rt + 1], None,
                        op0=ALU.mult)), [("vector", v)])
                    wh = [("vector", v)]
                    if hf == 0 and rt:
                        wh.append(("gpsimd", hq_st[rt - 1]))
                    v = p.emit("vector", (lambda E_, hf=hf: E_.tensor_copy(
                        hqb[:, hf * 4096:(hf + 1) * 4096], hqi[:])), wh)
                p2v[rt] = v
                hq_st[rt] = p.dma("gpsimd", hq_spill.ap()[rt * 128:(rt + 1) * 128, :], hqb[:], [("vector", v)])

            fc2mm, ffnev, ffn_st = {}, {}, {}
            lastev = {}
            last_ffo_st = 0
            for q4i in range(4):
                ldh = 0
                wq_ = [("gpsimd", hq_st[q4i * 4 + 3])]
                if q4i > 0:
                    wq_.append(("tensor", fc2mm[(q4i - 1, 1, 63)]))
                for oc in range(64):
                    ldh = p.dma("sync", hqt[:, oc * 512:(oc + 1) * 512],
                                hq_spill.ap()[q4i * 512:(q4i + 1) * 512, oc * 128:(oc + 1) * 128],
                                wq_ if oc == 0 else (), transpose=True)
                for eh in range(2):
                    for oc in range(64):
                        w = [("sync", ldh), ("gpsimd", w2t_done)]
                        if (q4i, eh, oc - 2) in fc2mm:
                            w.append(("tensor", fc2mm[(q4i, eh, oc - 2)]))
                        elif eh == 1 and oc < 2:
                            w.append(("tensor", fc2mm[(q4i, 0, 62 + oc)]))
                        ldw = p.dma("sync", w2c[oc % 2][:],
                                    w2t_scr.ap()[oc * 128:(oc + 1) * 128, eh * 1024:(eh + 1) * 1024], w)
                        mm = 0
                        for ri in range(4):
                            for h2 in range(2):
                                wmm = []
                                if h2 == 0:
                                    wmm.append(("sync", ldw))
                                    if oc == 0 and ri in lastev:
                                        wmm.append(("vector", lastev[ri]))
                                mm = p.emit("tensor", (lambda E_, ri=ri, oc=oc, h2=h2, i=oc % 2: E_.matmul(
                                    ps_y2[ri][:, h2 * 512:(h2 + 1) * 512],
                                    hqt[:, oc * 512 + ri * 128:oc * 512 + ri * 128 + 128],
                                    w2c[i][:, h2 * 512:(h2 + 1) * 512],
                                    start=(oc == 0), stop=(oc == 63), skip_group_check=True)), wmm)
                        fc2mm[(q4i, eh, oc)] = mm
                    for ri in range(4):
                        rt = q4i * 4 + ri
                        wev = [("tensor", fc2mm[(q4i, eh, 63)])]
                        if last_ffo_st:
                            wev.append(("gpsimd", last_ffo_st))
                        v = p.emit("vector", (lambda E_, ri=ri, rt=rt: E_.tensor_scalar(
                            ffo[:], ps_y2[ri][:], ev2[:, rt:rt + 1], None, op0=ALU.mult)), wev)
                        lastev[ri] = v
                        ffn_st[(rt, eh)] = p.dma(
                            "gpsimd",
                            ffn_spill.ap()[rt * 128:(rt + 1) * 128, eh * 1024:(eh + 1) * 1024],
                            ffo[:], [("vector", v)])
                        last_ffo_st = ffn_st[(rt, eh)]
            p.barrier_waits()
            p.flush(nc, sems)
        if int(os.environ.get("MK_PHASES", "9")) <= 5:
            return nc

        stOP = contextlib.ExitStack()
        with stOP:
            wos = stOP.enter_context(nc.sbuf_tensor("wos", [128, 4 * E], BF16))
            ctf = stOP.enter_context(nc.sbuf_tensor("ctf", [128, 4 * S], BF16))
            yob = [stOP.enter_context(nc.sbuf_tensor(f"yob{i}", [128, E], F32)) for i in range(2)]
            ps_o = [stOP.enter_context(nc.psum_tensor(f"ps_op{i}", [128, E], F32)) for i in range(2)]

            ldwo = p.dma("gpsimd", wos[:].rearrange("p (c w) -> p c w", w=E),
                         wo_in.rearrange("(c p) w -> p c w", p=128))
            ldcx = 0
            for hc in range(4):
                ldcx = p.dma("sync", ctf[:, hc * S:(hc + 1) * S],
                             ctx_scr.ap()[hc * 128:(hc + 1) * 128, :],
                             [("gpsimd", ctx_all)] if hc == 0 else ())
            rs_st = {}
            opv = {}
            for qt_i in range(64):
                i = qt_i % 2
                mm = 0
                wmm = [("sync", ldcx), ("gpsimd", ldwo)]
                if qt_i >= 2:
                    wmm.append(("vector", opv[qt_i - 2]))
                for ec in range(4):
                    for hc in range(4):
                        mm = p.emit("tensor", (lambda E_, i=i, ec=ec, hc=hc, qt_i=qt_i: E_.matmul(
                            ps_o[i][:, ec * 512:(ec + 1) * 512],
                            ctf[:, hc * S + qt_i * 128:hc * S + qt_i * 128 + 128],
                            wos[:, hc * E + ec * 512:hc * E + ec * 512 + 512],
                            start=(hc == 0), stop=(hc == 3), skip_group_check=True)),
                            wmm if (ec == 0 and hc == 0) else ())
                wv_ = [("tensor", mm)]
                if qt_i >= 2:
                    wv_.append(("gpsimd", rs_st[qt_i - 2]))
                v = p.emit("vector", (lambda E_, i=i: E_.tensor_copy(
                    yob[i][:], ps_o[i][:])), wv_)
                opv[qt_i] = v
                rs_st[qt_i] = p.dma("gpsimd", cc_rs_in.ap()[qt_i * 128:(qt_i + 1) * 128, :],
                                    yob[i][:], [("vector", v)])
            rsc = p.emit("gpsimd", (lambda E_: E_.collective_compute(
                "ReduceScatter", ALU.add, replica_groups=[[0, 1, 2, 3], [4, 5, 6, 7]],
                ins=[cc_rs_in.ap().opt()], outs=[cc_rs_out.ap().opt()])),
                [("gpsimd", rs_st[63])])
            p.emit("gpsimd", (lambda E_: None), [("gpsimd", rsc)], inc=0)
            p.barrier_waits()
            p.flush(nc, sems)
        if int(os.environ.get("MK_PHASES", "9")) <= 6:
            return nc

        stF = contextlib.ExitStack()
        with stF:
            xrr = [stF.enter_context(nc.sbuf_tensor(f"xrr{i}", [128, E], F32)) for i in range(2)]
            arr = [stF.enter_context(nc.sbuf_tensor(f"arr{i}", [128, E], F32)) for i in range(2)]
            frr = [stF.enter_context(nc.sbuf_tensor(f"frr{i}", [128, E], F32)) for i in range(2)]
            outb = [stF.enter_context(nc.sbuf_tensor(f"outb{i}", [128, E], F32)) for i in range(2)]
            brow = stF.enter_context(nc.sbuf_tensor("brow", [1, E], F32))
            b2row = stF.enter_context(nc.sbuf_tensor("b2row", [1, E], F32))
            bbc2 = stF.enter_context(nc.sbuf_tensor("bbc2", [128, E], F32))
            ps_b = stF.enter_context(nc.psum_tensor("ps_b", [128, 512], F32))

            p.dma("sync", brow[:], bo_in[:, :])
            ldb1 = p.dma("sync", b2row[:], b2_in[:, :])
            v = p.emit("vector", (lambda E_: E_.tensor_tensor(
                brow[:], brow[:], b2row[:], op=ALU.add)), [("sync", ldb1)])
            sc = 0
            for ec in range(4):
                wmb = [("vector", v)]
                if sc:
                    wmb.append(("scalar", sc))
                mm = p.emit("tensor", (lambda E_, ec=ec: E_.matmul(
                    ps_b[:], onesr[:], brow[:, ec * 512:(ec + 1) * 512], start=True, stop=True,
                    skip_group_check=True)), wmb)
                sc = p.emit("scalar", (lambda E_, ec=ec: E_.copy(
                    bbc2[:, ec * 512:(ec + 1) * 512], ps_b[:])), [("tensor", mm)])
            bias_done = sc
            y_st, oev = {}, {}
            for rt in range(NRT):
                w = [("vector", oev[rt - 2])] if rt >= 2 else [("gpsimd", rsc)]
                ldx = p.dma("sync", xrr[rt % 2][:], x_in[rt * 128:(rt + 1) * 128, :], w)
                lda = p.dma("sync", arr[rt % 2][:], cc_rs_out.ap()[rt * 128:(rt + 1) * 128, :],
                            [("gpsimd", rsc)] if rt == 0 else ())
                p.dma("sync", frr[rt % 2][:, 0:1024],
                      ffn_spill.ap()[rt * 128:(rt + 1) * 128, 0:1024],
                      [("gpsimd", ffn_st[(rt, 0)])])
                ldf = p.dma("sync", frr[rt % 2][:, 1024:2048],
                            ffn_spill.ap()[rt * 128:(rt + 1) * 128, 1024:2048],
                            [("gpsimd", ffn_st[(rt, 1)])])
                mo = os.environ.get("MK_OUT", "sum")
                if mo in ("xq", "aspill", "hq", "wq", "wqin", "w2t"):
                    srcmap = {"xq": xq_scr, "aspill": a_spill, "hq": hq_spill,
                              "wq": cc_w1_out.ap(), "wqin": cc_w1_in.ap(), "w2t": w2t_scr}
                    sm = srcmap[mo] if mo in ('wq', 'wqin') else srcmap[mo].ap()
                    rsel = rt % 8 if mo in ('wq', 'wqin') else rt
                    dbg = p.dma("gpsimd", outb[rt % 2][:],
                                sm[rsel * 128:(rsel + 1) * 128, 0:E],
                                [("sync", ldf), ("gpsimd", rsc)] +
                                ([("gpsimd", y_st[rt - 2])] if rt >= 2 else []))
                    v = p.emit("vector", (lambda E_, i=rt % 2: E_.tensor_scalar(
                        outb[i][:], outb[i][:], 1.0, None, op0=ALU.mult)), [("sync", dbg)])
                elif mo == "ffn":
                    v = p.emit("vector", (lambda E_, i=rt % 2: E_.tensor_copy(
                        outb[i][:], frr[i][:])),
                        [("sync", ldf), ("scalar", bias_done)] +
                        ([("gpsimd", y_st[rt - 2])] if rt >= 2 else []))
                elif mo == "attn":
                    v = p.emit("vector", (lambda E_, i=rt % 2: E_.tensor_copy(
                        outb[i][:], arr[i][:])),
                        [("sync", ldf), ("scalar", bias_done)] +
                        ([("gpsimd", y_st[rt - 2])] if rt >= 2 else []))
                else:
                    v = p.emit("vector", (lambda E_, i=rt % 2: E_.tensor_tensor(
                        outb[i][:], xrr[i][:], frr[i][:], op=ALU.add)),
                        [("sync", ldf), ("scalar", bias_done)] +
                        ([("gpsimd", y_st[rt - 2])] if rt >= 2 else []))
                    v = p.emit("vector", (lambda E_, i=rt % 2: E_.tensor_tensor(
                        outb[i][:], outb[i][:], arr[i][:], op=ALU.add)), [("vector", v)])
                    v = p.emit("vector", (lambda E_, i=rt % 2: E_.tensor_tensor(
                        outb[i][:], outb[i][:], bbc2[:], op=ALU.add)), [("vector", v)])
                oev[rt] = v
                y_st[rt] = p.dma("gpsimd", y_out[rt * 128:(rt + 1) * 128, :],
                                 outb[rt % 2][:], [("vector", v)])
            if os.environ.get("MK_DBG") == "late":
                p.dma("gpsimd", dbg_out[:, :],
                      cc_w_out.ap()[0:2048 * 2048].rearrange("(a b) -> a b", b=2048),
                      [("gpsimd", y_st[NRT - 1])])
            p.emit("gpsimd", (lambda E_: None), [("gpsimd", p.counts["gpsimd"])], inc=0)
            p.barrier_waits()
            p.flush(nc, sems)

    return nc


def _host_inputs(L0_attn_x, L0_attn_Wq, L0_attn_Wk, L0_attn_Wv, L0_attn_Wo,
                 L0_attn_bo, L0_ffn_w1, L0_ffn_b1, L0_ffn_ln_g, L0_ffn_ln_b,
                 L0_ffn_w2, L0_ffn_b2):
    x = np.asarray(L0_attn_x, dtype=np.float32).reshape(R, E)
    in_maps = []
    for c in range(NCORE):
        b, q4 = c // 4, c % 4
        rows = slice(b * S + q4 * RPC, b * S + (q4 + 1) * RPC)
        hcols = slice(q4 * 512, (q4 + 1) * 512)
        btab = np.zeros((128, 256), np.float64)
        pcol = np.arange(128)[:, None]
        for i in range(4):
            hglob = q4 * 4 + i
            slope = np.float64(np.float32(2.0 ** (-8.0 * (hglob + 1) / 16.0)))
            rel = np.arange(64)[None, :] - 60
            btab[:, i * 64:(i + 1) * 64] = slope * (pcol + 128 * rel - 511) / np.sqrt(128.0)
        masks = np.zeros((128, 4 * QB), np.float32)
        ff = np.arange(QB)[None, :]
        for rel in range(4):
            masks[:, rel * QB:(rel + 1) * QB] = ((pcol + 128 * rel) <= ff)
        in_maps.append({
            "x_rows": np.ascontiguousarray(x[rows]),
            "wq_s": np.ascontiguousarray(np.asarray(L0_attn_Wq, np.float32)[:, hcols]),
            "wk_s": np.ascontiguousarray(np.asarray(L0_attn_Wk, np.float32)[:, hcols]),
            "wv_s": np.ascontiguousarray(np.asarray(L0_attn_Wv, np.float32)[:, hcols]),
            "wo_s": np.ascontiguousarray(np.asarray(L0_attn_Wo, np.float32)[q4 * 512:(q4 + 1) * 512]),
            "bo": np.ascontiguousarray(np.asarray(L0_attn_bo, np.float32)[None, :]),
            "w1_s": np.ascontiguousarray(np.asarray(L0_ffn_w1, np.float32)[c * 1024:(c + 1) * 1024]),
            "w2_s": np.ascontiguousarray(np.asarray(L0_ffn_w2, np.float32)[c * 256:(c + 1) * 256]),
            "b1": np.ascontiguousarray(np.asarray(L0_ffn_b1, np.float32)[None, :]),
            "ln_g": np.ascontiguousarray(np.asarray(L0_ffn_ln_g, np.float32)[None, :]),
            "ln_b": np.ascontiguousarray(np.asarray(L0_ffn_ln_b, np.float32)[None, :]),
            "b2": np.ascontiguousarray(np.asarray(L0_ffn_b2, np.float32)[None, :]),
            "bias_tab": btab.astype(np.float32),
            "masks": masks.astype(ml_dtypes.bfloat16),
            "ones_col": np.ones((128, 1), np.float32),
            "ones_row": np.ones((1, 128), np.float32),
        })
    return in_maps


def kernel(**inputs):
    in_maps = _host_inputs(**inputs)
    nc = build_program()
    res = run_bass_kernel_spmd(nc, in_maps, core_ids=list(range(NCORE)))
    shards = [np.asarray(res.results[c]["y"]) for c in range(NCORE)]
    out = np.concatenate(shards, axis=0).reshape(B, S, E)
    return np.ascontiguousarray(out.astype(np.float32))
